# version 07 (final)
# baseline (speedup 1.0000x reference)
import numpy as np
from contextlib import ExitStack
import concourse.bass as bass
import concourse.mybir as mybir
from concourse.bass_utils import run_bass_kernel_spmd

F32 = mybir.dt.float32
BF16 = mybir.dt.bfloat16
AF = mybir.ActivationFunctionType
ALU = mybir.AluOpType
AX = mybir.AxisListType

NCORE = 8
TOK = 16384
D = 1024
TPC = TOK // NCORE
PAD = 8
HALO = 16
OWN0 = PAD + HALO
TB = OWN0 + TPC
NCH = 8
TCH = [(PAD, HALO), (OWN0, 512), (OWN0 + 512, 512), (OWN0 + 1024, 512), (OWN0 + 1536, 512)]
TCO = TCH[1:]
TBLK = [TCH[:3], TCH[3:]]

HEADS = 8
DH = 64
WIN = 512
INNER = 512
HID2 = 5632
HIDP = 2816
KHALO = 640
KTB = KHALO + TPC
VSTR = 8 * 65
NVC = KTB // 128

CONSTS = [0.0, 1.0, -1.0, 0.5, 30.0, 1e-5, 1e-12, 1.0 / 1024.0, 1e-30, -0.5]
_NSTAGE = 99
_DBG = False
_L1SUB = 0

ENGS = ("tensor", "vector", "scalar", "sync", "gpsimd")
OWN = {"tensor": "sT", "vector": "sV", "scalar": "sS", "sync": "sY", "gpsimd": "sG"}
DSEM = {"sync": "dY", "gpsimd": "dG", "scalar": "dS", "vector": "dV", "tensor": "dT"}
SEMNAMES = ["sT", "sV", "sS", "sY", "sG", "dY", "dG", "dS", "dV", "dT", "cc"]


class KB:
    def __init__(self, nc, block, sems):
        self.nc, self.block, self.sems = nc, block, sems
        self.cnt = {k: 0 for k in SEMNAMES}
        self.floor = {e: {k: 0 for k in SEMNAMES} for e in ENGS}

    def _waits(self, eng, wait):
        out = []
        for t in wait:
            if t is None:
                continue
            s, v = t
            if s == OWN[eng]:
                continue
            if v <= self.floor[eng][s]:
                continue
            self.floor[eng][s] = v
            out.append((s, v))
        return out

    def emit(self, eng, meth, *args, wait=(), inc=None, incby=1, **kw):
        ws = self._waits(eng, wait)
        tok = None
        if inc:
            self.cnt[inc] += incby
            tok = (inc, self.cnt[inc])
        sems = self.sems

        def fn(e, ws=ws, meth=meth, args=args, kw=kw, inc=inc, incby=incby):
            for (s, v) in ws:
                e.wait_ge(sems[s], v)
            r = getattr(e, meth)(*args, **kw)
            if inc:
                r.then_inc(sems[inc], incby)
        getattr(self.block, eng)(fn)
        return tok

    def dma(self, eng, out, in_, wait=(), **kw):
        return self.emit(eng, "dma_start", wait=wait, inc=DSEM[eng], incby=16,
                         out=out, in_=in_, **kw)

    def wait_only(self, eng, wait):
        ws = self._waits(eng, wait)
        if not ws:
            return
        sems = self.sems

        def fn(e, ws=ws):
            for (s, v) in ws:
                e.wait_ge(sems[s], v)
        getattr(self.block, eng)(fn)

    def barrier(self):
        snap = [(k, v) for k, v in self.cnt.items() if v > 0]
        for e in ENGS:
            self.wait_only(e, snap)

    def mm(self, bank_ap, pieces, wait_first=(), inc="sT"):
        n = len(pieces)
        tok = None
        for j, p in enumerate(pieces):
            l, r = p[0], p[1]
            w = list(p[2]) if len(p) > 2 and p[2] else []
            if j == 0:
                w = list(wait_first) + w
            tok = self.emit("tensor", "matmul", bank_ap, l, r,
                            start=(j == 0), stop=(j == n - 1),
                            wait=w, inc=(inc if j == n - 1 else None))
        return tok


def _ch(buf, c, start, width, tb=TB):
    return buf[:, c * tb + start: c * tb + start + width]


def build_program(nstage):
    nc = bass.Bass("TRN2", target_bir_lowering=False, debug=False)

    def inp(name, shape, dt=F32):
        return nc.dram_tensor(name, list(shape), dt, kind="ExternalInput").ap()

    xin = inp("xin", (D, TB))
    cst = inp("cst", (128, len(CONSTS)))
    cmask = inp("cmask", (128, 1))
    l0wrT = inp("l0wrT", (D, D)); l0wkT = inp("l0wkT", (D, D)); l0wvT = inp("l0wvT", (D, D))
    l0mur = inp("l0mur", (128, NCH)); l0muk = inp("l0muk", (128, NCH))
    l0wq = inp("l0wq", (D, DH)); l0wk = inp("l0wk", (D, DH)); l0wv = inp("l0wv", (D, DH))
    l1awq = inp("l1awq", (D, D)); l1awk = inp("l1awk", (D, D)); l1awv = inp("l1awv", (D, D))
    l1awo = inp("l1awo", (D, D))
    l1abq = inp("l1abq", (128, NCH)); l1abk = inp("l1abk", (128, NCH))
    l1abv = inp("l1abv", (128, NCH)); l1abo = inp("l1abo", (128, NCH))
    wqkv = inp("wqkv", (2, D, 3 * INNER))
    wo = inp("wo", (2, INNER, D))
    w1p = inp("w1p", (2, D, HID2))
    w2p = inp("w2p", (2, HIDP, D))
    b1p = inp("b1p", (2, 128, 44))
    b2r = inp("b2r", (2, 128, NCH))
    lng_pp = inp("lng_pp", (2, 128, NCH))
    lnbr = inp("lnbr", (2, 128, NCH))
    qscal = inp("qscal", (128, 2)); kscal = inp("kscal", (128, 2))
    rfc = inp("rfc", (64, TB)); rfs = inp("rfs", (64, TB))
    rlnb = inp("rlnb", (1, 64)); rlnbn = inp("rlnbn", (1, 64)); rpow = inp("rpow", (1, TB))
    mtril = inp("mtril", (128, 128)); mtriu = inp("mtriu", (128, 128))
    mrl = inp("mrl", (128, HALO)); mru = inp("mru", (128, HALO))
    bh16 = inp("bh16", (NCH, 128, 16))
    bh8 = inp("bh8", (4, 128, 8))
    bh8T = inp("bh8T", (4, 8, 128))
    bh16T = inp("bh16T", (NCH, 16, 128))
    rotm = inp("rotm", (128, 128))

    yT = nc.dram_tensor("yT", [D, TPC], F32, kind="ExternalOutput").ap()
    dbg = (nc.dram_tensor("dbg", [128, 512], F32, kind="ExternalOutput").ap()
           if _DBG else None)

    spill = nc.dram_tensor("spill", [128, NCH * TB], F32).ap()
    N1 = 64 * TPC + TPC * 64 + TPC + 16
    cc1i = nc.dram_tensor("cc1i", [1, N1], BF16).ap()
    cc1o = nc.dram_tensor("cc1o", [NCORE, N1], BF16).ap()
    cc2i = nc.dram_tensor("cc2i", [1, TOK], F32).ap()
    cc2o = nc.dram_tensor("cc2o", [NCORE, TOK], F32).ap()
    NKV = 4 * 128 * KHALO + 5 * 128 * VSTR
    kvis = [nc.dram_tensor("kvi%d" % i, [1, NKV], BF16).ap() for i in range(4)]
    kvos = [nc.dram_tensor("kvo%d" % i, [NCORE, NKV], BF16).ap() for i in range(4)]
    kvinst = [0]

    st = ExitStack()
    sems = {}
    for s in SEMNAMES:
        sems[s] = st.enter_context(nc.semaphore(s))

    XR = st.enter_context(nc.sbuf_tensor("XR", [128, NCH * TB], F32))
    ACT = st.enter_context(nc.sbuf_tensor("ACT", [128, NCH * TB], BF16))
    SC = st.enter_context(nc.sbuf_tensor("SC", [128, 2640], F32))
    SCB = st.enter_context(nc.sbuf_tensor("SCB", [128, 1536], BF16))
    CSTT = st.enter_context(nc.sbuf_tensor("CSTT", [128, len(CONSTS)], F32))
    CMK = st.enter_context(nc.sbuf_tensor("CMK", [128, 1], F32))
    ONEF = st.enter_context(nc.sbuf_tensor("ONEF", [128, 1], F32))
    ONEB = st.enter_context(nc.sbuf_tensor("ONEB", [128, 1], BF16))
    ONER = st.enter_context(nc.sbuf_tensor("ONER", [33, 128], F32))
    ONE64 = st.enter_context(nc.sbuf_tensor("ONE64", [1, 64], F32))
    LNS = st.enter_context(nc.sbuf_tensor("LNS", [33, TB], F32))
    LNG = st.enter_context(nc.sbuf_tensor("LNG", [128, 2 * NCH], F32))
    MU = st.enter_context(nc.sbuf_tensor("MU", [128, 4 * NCH], F32))
    LAB = st.enter_context(nc.sbuf_tensor("LAB", [128, 4 * NCH], F32))
    B1S = st.enter_context(nc.sbuf_tensor("B1S", [128, 88], F32))
    B2S = st.enter_context(nc.sbuf_tensor("B2S", [128, 2 * NCH], F32))
    LNBS = st.enter_context(nc.sbuf_tensor("LNBS", [128, 2 * NCH], F32))
    QSC = st.enter_context(nc.sbuf_tensor("QSC", [128, 4], F32))
    MTL = st.enter_context(nc.sbuf_tensor("MTL", [128, 128], BF16))
    MTU = st.enter_context(nc.sbuf_tensor("MTU", [128, 128], BF16))
    MRL = st.enter_context(nc.sbuf_tensor("MRL", [128, HALO], BF16))
    MRU = st.enter_context(nc.sbuf_tensor("MRU", [128, HALO], BF16))
    BH16S = st.enter_context(nc.sbuf_tensor("BH16S", [128, NCH * 16], BF16))
    BH8S = st.enter_context(nc.sbuf_tensor("BH8S", [128, 4 * 8], BF16))
    BH8TS = st.enter_context(nc.sbuf_tensor("BH8TS", [8, 4 * 128], BF16))
    BH16TS = st.enter_context(nc.sbuf_tensor("BH16TS", [16, NCH * 128], BF16))
    ROTS = st.enter_context(nc.sbuf_tensor("ROTS", [128, 128], BF16))
    QTC = st.enter_context(nc.sbuf_tensor("QTC", [128, TB], BF16))
    QTS = st.enter_context(nc.sbuf_tensor("QTS", [128, TB], BF16))
    KTC = st.enter_context(nc.sbuf_tensor("KTC", [128, TB], BF16))
    KTS = st.enter_context(nc.sbuf_tensor("KTS", [128, TB], BF16))

    pb = [st.enter_context(nc.psum_tensor("pb%d" % i, [128, 512], F32)) for i in range(8)]

    with st:
        with nc.Block() as block:
            kb = KB(nc, block, sems)
            _uid = [0]

            def sb(name, shape, dt):
                _uid[0] += 1
                return nc.sbuf_tensor("%s_u%d" % (name, _uid[0]), shape, dt)
            for j, v in enumerate(CONSTS):
                nc.const_aps.register(F32, v, CSTT[:, j:j + 1])

            PREVS = {}
            for _pe in ("sync", "scalar"):
                def _getpid(e, _pe=_pe):
                    pid = e.partition_id()
                    PREVS[_pe] = (pid + NCORE - 1) % NCORE
                getattr(block, _pe)(_getpid)
            tcst = kb.dma("sync", CSTT[:, :], cst[:, :])
            kb.dma("sync", CMK[:, :], cmask[:, :])
            for e in ENGS:
                kb.wait_only(e, [tcst])
            kb.emit("vector", "memset", ONEF[:, :], 1.0, inc="sV")
            kb.emit("vector", "memset", ONEB[:, :], 1.0, inc="sV")
            kb.emit("vector", "memset", ONER[0:1, :], 1.0, inc="sV")
            kb.emit("vector", "memset", ONER[32:33, :], 1.0, inc="sV")
            kb.emit("vector", "memset", ONE64[:, :], 1.0, inc="sV")
            kb.dma("sync", MU[:, 0:NCH], l0mur[:, :])
            kb.dma("sync", MU[:, 2 * NCH:3 * NCH], l0muk[:, :])
            kb.dma("sync", LAB[:, 0:NCH], l1abq[:, :])
            kb.dma("sync", LAB[:, NCH:2 * NCH], l1abk[:, :])
            kb.dma("sync", LAB[:, 2 * NCH:3 * NCH], l1abv[:, :])
            kb.dma("sync", LAB[:, 3 * NCH:4 * NCH], l1abo[:, :])
            for l in range(2):
                kb.dma("sync", B1S[:, 44 * l:44 * (l + 1)], b1p[l, :, :])
                kb.dma("sync", B2S[:, NCH * l:NCH * (l + 1)], b2r[l, :, :])
                kb.dma("sync", LNBS[:, NCH * l:NCH * (l + 1)], lnbr[l, :, :])
            for l in range(2):
                kb.dma("sync", LNG[:, NCH * l:NCH * (l + 1)], lng_pp[l, :, :])
            tqs = kb.dma("sync", QSC[:, 0:2], qscal[:, :])
            tks = kb.dma("sync", QSC[:, 2:4], kscal[:, :])
            kb.dma("gpsimd", MTL[:, :], mtril[:, :])
            kb.dma("gpsimd", MTU[:, :], mtriu[:, :])
            kb.dma("gpsimd", MRL[:, :], mrl[:, :])
            kb.dma("gpsimd", MRU[:, :], mru[:, :])
            kb.dma("gpsimd", BH16S[:, :].rearrange("p (c h) -> p c h", h=16),
                   bh16[:, :, :].rearrange("c p h -> p c h"))
            kb.dma("gpsimd", BH8S[:, :].rearrange("p (c h) -> p c h", h=8),
                   bh8[:, :, :].rearrange("c p h -> p c h"))
            kb.dma("gpsimd", BH8TS[:, :].rearrange("h (c p) -> h c p", p=128),
                   bh8T[:, :, :].rearrange("c h p -> h c p"))
            kb.dma("gpsimd", BH16TS[:, :].rearrange("h (c p) -> h c p", p=128),
                   bh16T[:, :, :].rearrange("c h p -> h c p"))
            kb.dma("gpsimd", ROTS[:, :], rotm[:, :])
            tmu = (DSEM["sync"], kb.cnt["dY"])
            kb.emit("vector", "tensor_scalar", MU[:, NCH:2 * NCH], MU[:, 0:NCH],
                    -1.0, 1.0, ALU.mult, ALU.add, wait=[tmu], inc="sV")
            kb.emit("vector", "tensor_scalar", MU[:, 3 * NCH:4 * NCH], MU[:, 2 * NCH:3 * NCH],
                    -1.0, 1.0, ALU.mult, ALU.add, inc="sV")
            for c in range(NCH):
                kb.dma("sync", XR[:, c * TB:(c + 1) * TB], xin[128 * c:128 * (c + 1), :])
            for c in range(NCH):
                kb.emit("vector", "memset", _ch(ACT, c, 0, PAD), 0.0, inc="sV")

            with ExitStack() as s0:
                RFC = s0.enter_context(sb("RFC", [64, TB], F32))
                RFS = s0.enter_context(sb("RFS", [64, TB], F32))
                RLB = s0.enter_context(sb("RLB", [1, 64], F32))
                RLBN = s0.enter_context(sb("RLBN", [1, 64], F32))
                RPW = s0.enter_context(sb("RPW", [1, TB], F32))
                t1 = kb.dma("sync", RFC[:, :], rfc[:, :])
                t2 = kb.dma("sync", RFS[:, :], rfs[:, :])
                t3 = kb.dma("sync", RLB[:, :], rlnb[:, :])
                t4 = kb.dma("sync", RLBN[:, :], rlnbn[:, :])
                t5 = kb.dma("sync", RPW[:, :], rpow[:, :])
                bk_tok = {6: None, 7: None}
                for (ts0, tw) in TCH:
                    sl = slice(ts0, ts0 + tw)
                    tsc = kb.mm(pb[6][0:64, 0:tw], [(RLB[:, :], RPW[:, sl])],
                                wait_first=[t3, t5, bk_tok[6]])
                    tisc = kb.mm(pb[7][0:64, 0:tw], [(RLBN[:, :], RPW[:, sl])],
                                 wait_first=[t4, t5, bk_tok[7]])
                    esc = kb.emit("scalar", "activation", SC[0:64, 0:tw], pb[6][0:64, 0:tw],
                                  AF.Exp, wait=[tsc], inc="sS")
                    eisc = kb.emit("scalar", "activation", SC[0:64, 512:512 + tw],
                                   pb[7][0:64, 0:tw], AF.Exp, wait=[tisc], inc="sS")
                    bk_tok[6], bk_tok[7] = esc, eisc
                    for (tab_c, tab_s, scl_off, wsc) in ((QTC, QTS, 0, esc),
                                                         (KTC, KTS, 512, eisc)):
                        sv = SC[0:64, scl_off:scl_off + tw]
                        c1 = kb.emit("scalar", "activation", tab_c[0:64, sl], RFC[:, sl],
                                     AF.Sin, wait=[t1], inc="sS")
                        s1 = kb.emit("scalar", "activation", tab_s[0:64, sl], RFS[:, sl],
                                     AF.Sin, wait=[t2], inc="sS")
                        m1 = kb.emit("vector", "tensor_tensor", tab_c[0:64, sl],
                                     tab_c[0:64, sl], sv, ALU.mult, wait=[c1, wsc], inc="sV")
                        m2 = kb.emit("vector", "tensor_tensor", tab_s[0:64, sl],
                                     tab_s[0:64, sl], sv, ALU.mult, wait=[s1, wsc], inc="sV")
                        kb.emit("vector", "tensor_copy", tab_c[64:128, sl], tab_c[0:64, sl],
                                wait=[m1], inc="sV")
                        kb.emit("vector", "tensor_copy", tab_s[64:128, sl], tab_s[0:64, sl],
                                wait=[m2], inc="sV")
            kb.barrier()

            CA = {v: j for j, v in enumerate(CONSTS)}

            def halo_fix():
                for c in range(NCH):
                    kb.emit("vector", "tensor_scalar", _ch(XR, c, PAD, HALO),
                            _ch(XR, c, PAD, HALO), CMK[:, 0:1], None, ALU.mult, inc="sV")

            def layernorm(src, out_buf, affine_l=None, src_is_bf=False):
                ones = ONEB if src_is_bf else ONEF
                stok = []
                b6 = None
                sq_gate = [None, None]
                for (ts0, tw) in TCH:
                    p1 = kb.mm(pb[6][0:1, 0:tw],
                               [(ones[:, :], _ch(src, c, ts0, tw)) for c in range(NCH)],
                               wait_first=[b6])
                    p2 = None
                    for c in range(NCH):
                        par = c % 2
                        tile = SCB[:, 512 * par:512 * par + tw]
                        sqt = kb.emit("scalar", "activation", tile, _ch(src, c, ts0, tw),
                                      AF.Square, wait=[sq_gate[par]], inc="sS")
                        p2 = kb.emit("tensor", "matmul", pb[7][0:1, 0:tw], ONEB[:, :], tile,
                                     start=(c == 0), stop=(c == NCH - 1), wait=[sqt],
                                     inc="sT")
                        sq_gate[par] = p2
                    c1 = kb.emit("vector", "tensor_copy", LNS[32:33, ts0:ts0 + tw],
                                 pb[6][0:1, 0:tw], wait=[p1], inc="sV")
                    c2 = kb.emit("vector", "tensor_copy", LNS[0:1, ts0:ts0 + tw],
                                 pb[7][0:1, 0:tw], wait=[p2], inc="sV")
                    b6 = c1
                    sq_gate = [c2, c2]
                    stok += [c1, c2]
                m1 = kb.emit("vector", "tensor_scalar", LNS[32:33, PAD:TB], LNS[32:33, PAD:TB],
                             1.0 / 1024.0, None, ALU.mult, wait=stok, inc="sV")
                m2 = kb.emit("vector", "tensor_scalar", LNS[0:1, PAD:TB], LNS[0:1, PAD:TB],
                             1.0 / 1024.0, None, ALU.mult, inc="sV")
                m3 = kb.emit("vector", "tensor_tensor", SC[0:1, PAD:TB], LNS[32:33, PAD:TB],
                             LNS[32:33, PAD:TB], ALU.mult, wait=[m1], inc="sV")
                m4 = kb.emit("vector", "tensor_tensor", LNS[0:1, PAD:TB], LNS[0:1, PAD:TB],
                             SC[0:1, PAD:TB], ALU.subtract, wait=[m2, m3], inc="sV")
                m5 = kb.emit("scalar", "activation", LNS[0:1, PAD:TB], LNS[0:1, PAD:TB],
                             AF.Sqrt, bias=1e-5, wait=[m4], inc="sS")
                m6 = kb.emit("vector", "reciprocal", LNS[0:1, PAD:TB], LNS[0:1, PAD:TB],
                             wait=[m5], inc="sV")
                m6b = kb.emit("vector", "tensor_copy", SC[32:33, 0:TB], LNS[0:1, 0:TB],
                              wait=[m6], inc="sV")
                m7 = kb.emit("vector", "tensor_tensor", LNS[32:33, PAD:TB], LNS[32:33, PAD:TB],
                             SC[32:33, PAD:TB], ALU.mult, wait=[m6b], inc="sV")
                m8 = m6
                b4 = None
                b5 = None
                aff_gate = None
                for (ts0, tw) in TCH:
                    for c in range(NCH):
                        t1 = kb.mm(pb[4][:, 0:tw],
                                   [(ONER[0:1, :], LNS[0:1, ts0:ts0 + tw], [m8])],
                                   wait_first=[b4])
                        t2 = kb.mm(pb[5][:, 0:tw],
                                   [(ONER[32:33, :], LNS[32:33, ts0:ts0 + tw], [m7])],
                                   wait_first=[b5])
                        par = c % 2
                        tmp = SCB[:, 512 * par:512 * par + tw]
                        v1 = kb.emit("vector", "tensor_tensor", tmp, _ch(src, c, ts0, tw),
                                     pb[4][:, 0:tw], ALU.mult, wait=[t1], inc="sV")
                        if affine_l is None:
                            v2 = kb.emit("vector", "scalar_tensor_tensor",
                                         _ch(out_buf, c, ts0, tw), tmp,
                                         CSTT[:, CA[0.0]:CA[0.0] + 1], pb[5][:, 0:tw],
                                         ALU.add, ALU.subtract, wait=[t2, v1], inc="sV")
                        else:
                            tmp2 = SCB[:, 1024:1024 + tw]
                            v2a = kb.emit("vector", "scalar_tensor_tensor", tmp2, tmp,
                                          CSTT[:, CA[0.0]:CA[0.0] + 1], pb[5][:, 0:tw],
                                          ALU.add, ALU.subtract,
                                          wait=[t2, v1, aff_gate], inc="sV")
                            v2 = kb.emit("vector", "tensor_scalar",
                                         _ch(out_buf, c, ts0, tw), tmp2,
                                         LNG[:, affine_l * NCH + c:affine_l * NCH + c + 1],
                                         LNBS[:, affine_l * NCH + c:affine_l * NCH + c + 1],
                                         ALU.mult, ALU.add, wait=[v2a], inc="sV")
                            aff_gate = v2
                        b4, b5 = v1, v2

            def l0_ffn():
                layernorm(XR, ACT)
                kb.barrier()
                with ExitStack() as ss:
                    TW = 1040
                    MIX = ss.enter_context(sb("MIX", [128, NCH * TW], BF16))
                    RB = ss.enter_context(sb("RB", [128, NCH * TW], BF16))
                    KA = ss.enter_context(sb("KA", [128, NCH * TW], BF16))
                    WD = ss.enter_context(sb("WD", [128, 2 * NCH * 128], BF16))
                    for blk in TBLK:
                        b0 = blk[0][0]
                        bw = sum(w for (_, w) in blk)

                        def mixes(mu_off):
                            toks = []
                            for c in range(NCH):
                                tA = kb.emit("vector", "tensor_scalar",
                                             _ch(MIX, c, 0, bw, TW), _ch(ACT, c, b0, bw),
                                             MU[:, mu_off + c:mu_off + c + 1], None,
                                             ALU.mult, inc="sV")
                                toks.append(kb.emit(
                                    "vector", "scalar_tensor_tensor",
                                    _ch(MIX, c, 0, bw, TW), _ch(ACT, c, b0 - 1, bw),
                                    MU[:, NCH + mu_off + c:NCH + mu_off + c + 1],
                                    _ch(MIX, c, 0, bw, TW), ALU.mult, ALU.add,
                                    wait=[tA], inc="sV"))
                            return toks

                        def mm_block(wsrc, rhsbuf, rhs_tok, outfn):
                            wd_used = [None, None]
                            bank_tok = [None, None]
                            for ncx in range(NCH):
                                slot = ncx % 2
                                wt = kb.dma("gpsimd",
                                            WD[:, slot * 1024:(slot + 1) * 1024]
                                            .rearrange("p (c f) -> p c f", f=128),
                                            wsrc[:, 128 * ncx:128 * (ncx + 1)]
                                            .rearrange("(c p) f -> p c f", p=128),
                                            wait=[wd_used[slot]])
                                for (ts0, tw) in blk:
                                    bank = pb[slot][:, 0:tw]
                                    mt = kb.mm(bank,
                                               [(WD[:, slot * 1024 + 128 * cc:
                                                    slot * 1024 + 128 * (cc + 1)],
                                                 _ch(rhsbuf, cc, ts0 - b0, tw, TW))
                                                for cc in range(NCH)],
                                               wait_first=[wt, bank_tok[slot]] + rhs_tok)
                                    ot = outfn(ncx, ts0, tw, bank, mt)
                                    bank_tok[slot] = ot
                                    wd_used[slot] = ot

                        mt0 = mixes(0)

                        def out_r(ncx, ts0, tw, bank, mt):
                            return kb.emit("scalar", "activation",
                                           _ch(RB, ncx, ts0 - b0, tw, TW), bank,
                                           AF.Sigmoid, wait=[mt], inc="sS")
                        mm_block(l0wrT, MIX, mt0, out_r)
                        kb.barrier()
                        mt1 = mixes(2 * NCH)
                        sq_gate = [None, None]

                        def out_k(ncx, ts0, tw, bank, mt):
                            par = ncx % 2
                            tile = SCB[:, 512 * par:512 * par + tw]
                            rl = kb.emit("scalar", "activation", tile, bank, AF.Relu,
                                         wait=[mt, sq_gate[par]], inc="sS")
                            t = kb.emit("vector", "tensor_tensor",
                                        _ch(KA, ncx, ts0 - b0, tw, TW), tile, tile,
                                        ALU.mult, wait=[rl], inc="sV")
                            sq_gate[par] = t
                            return t
                        mm_block(l0wkT, MIX, mt1, out_k)
                        kb.barrier()

                        def out_v(ncx, ts0, tw, bank, mt):
                            t1 = kb.emit("vector", "tensor_tensor", SC[:, 2072:2072 + tw],
                                         bank, _ch(RB, ncx, ts0 - b0, tw, TW), ALU.mult,
                                         wait=[mt], inc="sV")
                            return kb.emit("vector", "scalar_tensor_tensor",
                                           _ch(XR, ncx, ts0, tw), SC[:, 2072:2072 + tw],
                                           0.5, _ch(XR, ncx, ts0, tw), ALU.mult, ALU.add,
                                           wait=[t1], inc="sV")
                        mm_block(l0wvT, KA, [], out_v)
                        kb.barrier()
                halo_fix()
                kb.barrier()

            def l0_attn():
                layernorm(XR, ACT)
                kb.barrier()
                with ExitStack() as ss:
                    Q0 = ss.enter_context(sb("Q0", [128, TB], BF16))
                    K0 = ss.enter_context(sb("K0", [64, TB], BF16))
                    VOWN = ss.enter_context(sb("VOWN", [128, 16 * 64], BF16))
                    WQD = ss.enter_context(sb("WQD", [128, 3 * 512], BF16))
                    wt = []
                    for wi, w in enumerate((l0wq, l0wk, l0wv)):
                        wt.append(kb.dma("gpsimd",
                                         WQD[:, wi * 512:(wi + 1) * 512]
                                         .rearrange("p (c f) -> p c f", f=64),
                                         w[:, :].rearrange("(c p) f -> p c f", p=128)))
                    bk_tok = {0: None, 1: None, 2: None, 3: None}
                    for (ts0, tw) in TCH:
                        for wi, dst in ((0, Q0), (1, K0)):
                            bank = pb[wi][0:64, 0:tw]
                            mt = kb.mm(bank,
                                       [(WQD[:, wi * 512 + 64 * cc:wi * 512 + 64 * (cc + 1)],
                                         _ch(ACT, cc, ts0, tw)) for cc in range(NCH)],
                                       wait_first=[wt[wi], bk_tok[wi]])
                            e1 = kb.emit("scalar", "activation", dst[0:64, ts0:ts0 + tw],
                                         bank, AF.Copy, wait=[mt], inc="sS")
                            if wi == 0:
                                e1 = kb.emit("scalar", "activation",
                                             Q0[64:128, ts0:ts0 + tw], bank, AF.Copy,
                                             inc="sS")
                            bk_tok[wi] = e1
                    for j in range(16):
                        ts0 = OWN0 + 128 * j
                        for wi, bidx in ((2, 2), (1, 3)):
                            bank = pb[bidx][:, 0:64]
                            mt = kb.mm(bank,
                                       [(_ch(ACT, cc, ts0, 128),
                                         WQD[:, wi * 512 + 64 * cc:wi * 512 + 64 * (cc + 1)])
                                        for cc in range(NCH)],
                                       wait_first=[wt[wi], bk_tok[bidx]])
                            if wi == 2:
                                bk_tok[bidx] = kb.emit("scalar", "activation",
                                                       VOWN[:, 64 * j:64 * (j + 1)], bank,
                                                       AF.Copy, wait=[mt], inc="sS")
                            else:
                                sq = kb.emit("scalar", "activation", SCB[:, 0:64],
                                             bank, AF.Square, wait=[mt], inc="sS")
                                bk_tok[bidx] = kb.emit("vector", "tensor_reduce",
                                                       SC[:, 2072 + j:2073 + j],
                                                       SCB[:, 0:64], AX.X, ALU.add,
                                                       wait=[sq], inc="sV")
                    qm = []
                    b4t = None
                    qready = (OWN["scalar"], kb.cnt["sS"])
                    kvready = (OWN["vector"], kb.cnt["sV"])
                    for ti, (ts0, tw) in enumerate(TCO):
                        sq = kb.emit("vector", "tensor_tensor", SCB[0:64, 1024:1024 + tw],
                                     Q0[0:64, ts0:ts0 + tw], Q0[0:64, ts0:ts0 + tw],
                                     ALU.mult, wait=[qready], inc="sV")
                        mt = kb.mm(pb[4][0:1, 0:tw],
                                   [(ONEB[0:64, :], SCB[0:64, 1024:1024 + tw], [sq])],
                                   wait_first=[b4t])
                        b4t = kb.emit("vector", "tensor_reduce", SC[0:1, 2605 + ti:2606 + ti],
                                      pb[4][0:1, 0:tw], AX.X, ALU.max, wait=[mt], inc="sV")
                        qm.append(b4t)
                    rmax = kb.emit("vector", "tensor_reduce", SC[0:1, 2609:2610],
                                   SC[0:1, 2605:2609], AX.X, ALU.max, wait=qm, inc="sV")
                    tqm = kb.emit("scalar", "activation", SC[0:1, 2610:2611],
                                  SC[0:1, 2609:2610], AF.Sqrt, wait=[rmax], inc="sS")
                    kb.barrier()
                    d1 = kb.dma("sync", cc1i[0, 0:64 * TPC].rearrange("(d t) -> d t", d=64),
                                K0[0:64, OWN0:TB], wait=[qready])
                    d2 = kb.dma("sync", cc1i[0, 64 * TPC:128 * TPC]
                                .rearrange("(c p f) -> p c f", p=128, f=64),
                                VOWN[:, :].rearrange("p (c f) -> p c f", f=64),
                                wait=[qready])
                    d3 = kb.dma("gpsimd", cc1i[0, 128 * TPC:128 * TPC + TPC]
                                .rearrange("(p j) -> p j", j=16), SC[:, 2072:2088],
                                wait=[kvready])
                    d4 = kb.dma("gpsimd", cc1i[0:1, 128 * TPC + TPC:128 * TPC + TPC + 1],
                                SC[0:1, 2610:2611], wait=[tqm])
                    ct = kb.emit("gpsimd", "collective_compute", "AllGather", ALU.bypass,
                                 replica_groups=[list(range(NCORE))],
                                 ins=[cc1i[:, :]], outs=[cc1o[:, :]],
                                 wait=[d1, d2, d3, d4], inc="cc")
                    ung = []
                    for r in range(NCORE):
                        half = r // 4
                        ung.append(kb.dma(
                            "sync", ACT[64 * half:64 * half + 64,
                                        2048 * (r % 4):2048 * (r % 4 + 1)],
                            cc1o[r, 0:64 * TPC].rearrange("(d t) -> d t", d=64), wait=[ct]))
                        ung.append(kb.dma(
                            "sync",
                            ACT[:, 8192:16512].rearrange("p (c f) -> p c f", f=65)
                            [:, 16 * r:16 * (r + 1), 0:64],
                            cc1o[r, 64 * TPC:128 * TPC]
                            .rearrange("(c p f) -> p c f", p=128, f=64), wait=[ct]))
                        ung.append(kb.dma(
                            "gpsimd", SC[:, 2088 + 16 * r:2088 + 16 * (r + 1)],
                            cc1o[r, 128 * TPC:128 * TPC + TPC]
                            .rearrange("(p j) -> p j", j=16), wait=[ct]))
                    tq8 = kb.dma("gpsimd", SC[0:1, 2611:2619],
                                 cc1o[:, 128 * TPC + TPC:128 * TPC + TPC + 1]
                                 .rearrange("r x -> x r"), wait=[ct],
                                 allow_slow_non_contiguous=True)
                    kb.emit("vector", "memset",
                            ACT[:, 8192:16512].rearrange("p (c f) -> p c f", f=65)
                            [:, :, 64:65], 1.0, inc="sV")
                    t1 = kb.emit("vector", "tensor_reduce", SC[0:1, 2619:2620],
                                 SC[0:1, 2611:2619], AX.X, ALU.max, wait=[tq8], inc="sV")
                    t3 = kb.mm(pb[4][:, 0:1], [(ONER[0:1, :], SC[0:1, 2619:2620], [t1])])
                    t4 = kb.emit("vector", "tensor_copy", SC[:, 2620:2621], pb[4][:, 0:1],
                                 wait=[t3], inc="sV")
                    t5 = kb.emit("scalar", "activation", SC[:, 2088:2216], SC[:, 2088:2216],
                                 AF.Sqrt, wait=ung, inc="sS")
                    t6 = kb.emit("vector", "tensor_scalar", SC[:, 2088:2216],
                                 SC[:, 2088:2216], SC[:, 2620:2621], -1.0,
                                 ALU.mult, ALU.mult, wait=[t4, t5], inc="sV")
                    kb.emit("vector", "tensor_scalar", SC[:, 2088:2216], SC[:, 2088:2216],
                            30.0, None, ALU.add, wait=[t6], inc="sV")
                    kb.barrier()
                    exg = [None, None, None]
                    _BB = {0: 0, 1: 1, 2: 5}
                    red_prev = None
                    for kc in range(128):
                        half = kc // 64
                        lhs = ACT[64 * half:64 * half + 64,
                                  128 * (kc % 64):128 * (kc % 64) + 128]
                        acc = []
                        for ti, (ts0, tw) in enumerate(TCO):
                            par = (kc * 4 + ti) % 3
                            bank = pb[_BB[par]][:, 0:tw]
                            mt = kb.mm(bank, [(lhs, Q0[64 * half:64 * half + 64,
                                                       ts0:ts0 + tw])],
                                       wait_first=[exg[par]])
                            et = kb.emit("scalar", "activation",
                                         SCB[:, 512 * par:512 * par + tw], bank, AF.Exp,
                                         bias=SC[:, 2088 + kc:2089 + kc],
                                         accum_out=SC[:, 2217 + ti:2218 + ti],
                                         wait=[mt] + ([red_prev] if ti == 0 else []),
                                         inc="sS")
                            exg[par] = et
                            acc.append(et)
                        red_prev = kb.emit("vector", "tensor_reduce",
                                           SC[:, 2221 + kc:2222 + kc], SC[:, 2217:2221],
                                           AX.X, ALU.add, wait=acc, inc="sV")
                    kb.barrier()
                    d5 = kb.dma("sync", cc2i[0, :].rearrange("(p j) -> p j", j=128),
                                SC[:, 2221:2349])
                    ct2 = kb.emit("gpsimd", "collective_compute", "AllGather", ALU.bypass,
                                  replica_groups=[list(range(NCORE))],
                                  ins=[cc2i[:, :]], outs=[cc2o[:, :]], wait=[d5], inc="cc")
                    at = kb.dma("sync", SC[:, 2349:2477],
                                cc2o[0, :].rearrange("(p j) -> p j", j=128), wait=[ct2])
                    for r in range(1, NCORE):
                        dt_ = kb.dma("sync", SC[:, 2477:2605],
                                     cc2o[r, :].rearrange("(p j) -> p j", j=128),
                                     wait=[ct2, at])
                        at = kb.emit("vector", "tensor_tensor", SC[:, 2349:2477],
                                     SC[:, 2349:2477], SC[:, 2477:2605], ALU.add,
                                     wait=[dt_], inc="sV")
                    l1t = kb.emit("scalar", "activation", SC[:, 2477:2605], SC[:, 2349:2477],
                                  AF.Ln, bias=1e-30, wait=[at], inc="sS")
                    bC = kb.emit("vector", "tensor_tensor", SC[:, 2088:2216],
                                 SC[:, 2088:2216], SC[:, 2477:2605], ALU.subtract,
                                 wait=[l1t], inc="sV")
                    if dbg is not None:
                        kb.dma("sync", dbg[:, 0:128], SC[:, 2088:2216], wait=[bC])
                        kb.dma("sync", dbg[:, 128:256], SC[:, 2349:2477], wait=[bC])
                        kb.dma("sync", dbg[:, 256:384], SC[:, 2221:2349], wait=[bC])
                        kb.dma("sync", dbg[:, 384:400], SC[:, 2072:2088], wait=[bC])
                        kb.dma("sync", dbg[:, 400:401], SC[:, 2620:2621], wait=[bC],
                               allow_slow_non_contiguous=True)
                    kb.barrier()
                    vg = ACT[:, 8192:16512].rearrange("p (c f) -> p c f", f=65)
                    resid_gate = None
                    for (ts0, tw) in TCH:
                        oB = pb[2][0:64, 0:tw]
                        exg2 = [None, None, None]
                        av_last = None
                        for kc in range(128):
                            half = kc // 64
                            par = kc % 3
                            lhs = ACT[64 * half:64 * half + 64,
                                      128 * (kc % 64):128 * (kc % 64) + 128]
                            bank = pb[_BB[par]][:, 0:tw]
                            mt = kb.mm(bank, [(lhs, Q0[64 * half:64 * half + 64,
                                                       ts0:ts0 + tw])],
                                       wait_first=[exg2[par]] +
                                       ([resid_gate] if kc == 0 else []))
                            et = kb.emit("scalar", "activation",
                                         SCB[:, 512 * par:512 * par + tw], bank, AF.Exp,
                                         bias=SC[:, 2088 + kc:2089 + kc], wait=[mt],
                                         inc="sS")
                            exg2[par] = et
                            av_last = kb.emit("tensor", "matmul", oB, vg[:, kc, 0:64],
                                              SCB[:, 512 * par:512 * par + tw],
                                              start=(kc == 0), stop=(kc == 127),
                                              wait=[et], inc="sT" if kc == 127 else None)
                        fin = []
                        for c in range(NCH):
                            f1 = kb.emit("vector", "scalar_tensor_tensor",
                                         XR[0:64, c * TB + ts0:c * TB + ts0 + tw], oB, 1.0,
                                         XR[0:64, c * TB + ts0:c * TB + ts0 + tw],
                                         ALU.mult, ALU.add, wait=[av_last], inc="sV")
                            f2 = kb.emit("vector", "scalar_tensor_tensor",
                                         XR[64:128, c * TB + ts0:c * TB + ts0 + tw], oB,
                                         1.0, XR[64:128, c * TB + ts0:c * TB + ts0 + tw],
                                         ALU.mult, ALU.add, inc="sV")
                            fin += [f1, f2]
                        resid_gate = fin[-1]
                halo_fix()
                kb.barrier()

            def l1_layer(l):
                kvi = kvis[kvinst[0]]
                kvo = kvos[kvinst[0]]
                kvinst[0] += 1
                for (ts0, tw) in TCH:
                    for c in range(NCH):
                        src = _ch(XR, c, ts0 - (1 if c >= 4 else 0), tw)
                        kb.emit("scalar", "activation", _ch(ACT, c, ts0, tw), src, AF.Copy,
                                inc="sS")
                kb.barrier()
                with ExitStack() as ss:
                    QB = ss.enter_context(sb("QB", [128, 4 * TB], BF16))
                    KBf = ss.enter_context(sb("KBf", [128, 4 * KTB], BF16))
                    with ExitStack() as s2:
                        WQD = s2.enter_context(sb("WQD", [128, 2 * 1024], BF16))
                        QRW = s2.enter_context(sb("QRW", [128, TB], BF16))
                        TMP = s2.enter_context(sb("TMP", [128, TB], BF16))
                        NRM = s2.enter_context(sb("NRM", [8, TB], BF16))

                        def qk_proc(qc, dstbuf, dst_off, tabc, tabs, scidx, tcs, col0,
                                    raw_toks):
                            nt = []
                            b4 = None
                            sq_gate = None
                            for (ts0, tw) in tcs:
                                sq = kb.emit("vector", "tensor_tensor", SCB[:, 1024:1024 + tw],
                                             QRW[:, ts0:ts0 + tw], QRW[:, ts0:ts0 + tw],
                                             ALU.mult, wait=raw_toks + [sq_gate], inc="sV")
                                mt = kb.mm(pb[4][0:8, 0:tw],
                                           [(BH8S[:, 8 * qc:8 * (qc + 1)],
                                             SCB[:, 1024:1024 + tw], [sq])],
                                           wait_first=[b4])
                                b4 = kb.emit("vector", "tensor_copy", NRM[0:8, ts0:ts0 + tw],
                                             pb[4][0:8, 0:tw], wait=[mt], inc="sV")
                                sq_gate = mt
                                nt.append(b4)
                            n1 = kb.emit("scalar", "activation", NRM[0:8, col0:TB],
                                         NRM[0:8, col0:TB], AF.Sqrt, wait=nt, inc="sS")
                            n2 = kb.emit("vector", "tensor_scalar_max", NRM[0:8, col0:TB],
                                         NRM[0:8, col0:TB], 1e-12, wait=[n1], inc="sV")
                            with nc.allow_low_precision(reason="bf16 recip of l2norm"):
                                n3 = kb.emit("vector", "reciprocal", NRM[0:8, col0:TB],
                                             NRM[0:8, col0:TB], wait=[n2], inc="sV")
                            out_toks = []
                            b5 = None
                            b6 = None
                            for (ts0, tw) in tcs:
                                bt = kb.mm(pb[5][:, 0:tw],
                                           [(BH8TS[0:8, 128 * qc:128 * (qc + 1)],
                                             NRM[0:8, ts0:ts0 + tw], [n3])],
                                           wait_first=[b5])
                                q1 = kb.emit("vector", "tensor_tensor", QRW[:, ts0:ts0 + tw],
                                             QRW[:, ts0:ts0 + tw], pb[5][:, 0:tw], ALU.mult,
                                             wait=[bt], inc="sV")
                                b5 = q1
                                q2 = kb.emit("vector", "tensor_scalar", QRW[:, ts0:ts0 + tw],
                                             QRW[:, ts0:ts0 + tw], QSC[:, scidx:scidx + 1],
                                             None, ALU.mult, wait=[tqs, tks], inc="sV")
                                rt_ = kb.mm(pb[6][:, 0:tw], [(ROTS[:, :], QRW[:, ts0:ts0 + tw],
                                                              [q2])], wait_first=[b6])
                                rc = kb.emit("scalar", "activation", TMP[:, ts0:ts0 + tw],
                                             pb[6][:, 0:tw], AF.Copy, wait=[rt_], inc="sS")
                                b6 = rc
                                dsl = dstbuf[:, dst_off + (ts0 - col0):
                                             dst_off + (ts0 - col0) + tw]
                                m1 = kb.emit("vector", "tensor_tensor", dsl,
                                             QRW[:, ts0:ts0 + tw], tabc[:, ts0:ts0 + tw],
                                             ALU.mult, inc="sV")
                                m2 = kb.emit("vector", "tensor_tensor", TMP[:, ts0:ts0 + tw],
                                             TMP[:, ts0:ts0 + tw], tabs[:, ts0:ts0 + tw],
                                             ALU.mult, wait=[rc], inc="sV")
                                out_toks.append(kb.emit("vector", "tensor_tensor", dsl, dsl,
                                                        TMP[:, ts0:ts0 + tw], ALU.add,
                                                        wait=[m1, m2], inc="sV"))
                            return out_toks

                        wd_used = [None, None]
                        prev_chunk = []
                        for ncx in range(8):
                            slot = ncx % 2
                            wtk = kb.dma("gpsimd", WQD[:, 1024 * slot:1024 * (slot + 1)]
                                         .rearrange("p (c f) -> p c f", f=128),
                                         wqkv[l][:, 128 * ncx:128 * (ncx + 1)]
                                         .rearrange("(c p) f -> p c f", p=128),
                                         wait=[wd_used[slot]])
                            tcs = TCH if ncx < 4 else TCO
                            raw = []
                            bk = None
                            for (ts0, tw) in tcs:
                                bank = pb[slot][:, 0:tw]
                                mt = kb.mm(bank,
                                           [(WQD[:, 1024 * slot + 128 * cc:
                                                 1024 * slot + 128 * (cc + 1)],
                                             _ch(ACT, cc, ts0, tw)) for cc in range(NCH)],
                                           wait_first=[wtk, bk])
                                bk = kb.emit("scalar", "activation", QRW[:, ts0:ts0 + tw],
                                             bank, AF.Copy, wait=[mt] + prev_chunk,
                                             inc="sS")
                                raw.append(bk)
                            if ncx < 4:
                                toks = qk_proc(ncx, QB, ncx * TB + PAD, QTC, QTS, l,
                                               TCH, PAD, raw)
                            else:
                                toks = qk_proc(ncx - 4, KBf, (ncx - 4) * KTB + KHALO,
                                               KTC, KTS, 2 + l, TCO, OWN0, raw)
                            wd_used[slot] = toks[-1]
                            prev_chunk = toks
                    VB = ss.enter_context(sb("VB", [128, NVC * VSTR], BF16))
                    EXQ = ss.enter_context(sb("EXQ", [128, 2 * 640], BF16))
                    WQD = ss.enter_context(sb("WQD2", [128, 2 * 1024], BF16))
                    wd_used = [None, None]
                    vtoks = []
                    for vc in range(4):
                        ncx = 8 + vc
                        slot = vc % 2
                        wtk = kb.dma("gpsimd", WQD[:, 1024 * slot:1024 * (slot + 1)]
                                     .rearrange("p (c f) -> p c f", f=128),
                                     wqkv[l][:, 128 * ncx:128 * (ncx + 1)]
                                     .rearrange("(c p) f -> p c f", p=128),
                                     wait=[wd_used[slot]])
                        bk = None
                        for j in range(16):
                            ts0 = OWN0 + 128 * j
                            bank = pb[slot][:, 0:128]
                            mt = kb.mm(bank,
                                       [(_ch(ACT, cc, ts0, 128),
                                         WQD[:, 1024 * slot + 128 * cc:
                                             1024 * slot + 128 * (cc + 1)])
                                        for cc in range(NCH)], wait_first=[wtk, bk])
                            bk = kb.emit("scalar", "activation",
                                         VB[:, (5 + j) * VSTR + 130 * vc:
                                            (5 + j) * VSTR + 130 * (vc + 1)]
                                         .rearrange("p (h f) -> p h f", f=65)[:, :, 0:64],
                                         bank.rearrange("p (h f) -> p h f", f=64),
                                         AF.Copy, wait=[mt], inc="sS")
                            vtoks.append(bk)
                        wd_used[slot] = bk
                    on1 = kb.emit("vector", "memset",
                                  VB[:, 5 * VSTR:].rearrange("p (x f) -> p x f", f=65)
                                  [:, :, 64:65], 1.0, inc="sV")
                    dk = []
                    for qc in range(4):
                        dk.append(kb.dma(
                            "sync", kvi[0, qc * 128 * KHALO:(qc + 1) * 128 * KHALO]
                            .rearrange("(p f) -> p f", p=128),
                            KBf[:, qc * KTB + KTB - KHALO:(qc + 1) * KTB],
                            wait=prev_chunk))
                    dk.append(kb.dma(
                        "sync", kvi[0, 4 * 128 * KHALO:NKV]
                        .rearrange("(c p f) -> p c f", p=128, f=VSTR),
                        VB[:, 16 * VSTR:21 * VSTR].rearrange("p (c f) -> p c f", f=VSTR),
                        wait=vtoks + [on1]))
                    ct = kb.emit("gpsimd", "collective_compute", "AllGather", ALU.bypass,
                                 replica_groups=[list(range(NCORE))],
                                 ins=[kvi[:, :]], outs=[kvo[:, :]], wait=dk, inc="cc")

                    ccv = kb.cnt["cc"]
                    qeng = ["sync", "scalar", "sync", "scalar"][kvinst[0] - 1]
                    qsem = DSEM[qeng]

                    def recv(e):
                        prev = PREVS[qeng]
                        e.wait_ge(sems["cc"], ccv)
                        e.dma_start(
                            out=KBf[:, 0:4 * KTB].rearrange("p (c f) -> p c f", f=KTB)
                            [:, :, 0:KHALO],
                            in_=kvo[prev, 0:4 * 128 * KHALO]
                            .rearrange("(c p f) -> p c f", p=128, f=KHALO)
                        ).then_inc(sems[qsem], 16)
                        e.dma_start(
                            out=VB[:, 0:5 * VSTR].rearrange("p (c f) -> p c f", f=VSTR),
                            in_=kvo[prev, 4 * 128 * KHALO:NKV]
                            .rearrange("(c p f) -> p c f", p=128, f=VSTR)
                        ).then_inc(sems[qsem], 16)
                    kb.floor[qeng]["cc"] = max(kb.floor[qeng]["cc"], ccv)
                    getattr(block, qeng)(recv)
                    kb.cnt[qsem] += 32
                    trecv = (qsem, kb.cnt[qsem])
                    zt = []
                    for qc in range(4):
                        zt.append(kb.emit("vector", "tensor_scalar",
                                          KBf[:, qc * KTB:qc * KTB + KHALO],
                                          KBf[:, qc * KTB:qc * KTB + KHALO], CMK[:, 0:1],
                                          None, ALU.mult, wait=[trecv], inc="sV"))
                    zt.append(kb.emit("vector", "tensor_scalar", VB[:, 0:5 * VSTR],
                                      VB[:, 0:5 * VSTR], CMK[:, 0:1], None, ALU.mult,
                                      wait=[trecv], inc="sV"))
                    zt.append(kb.emit("vector", "memset",
                                      VB[:, 0:5 * VSTR].rearrange("p (x f) -> p x f", f=65)
                                      [:, :, 64:65], 1.0, inc="sV"))
                    kb.barrier()
                    ichunks = [(PAD, HALO, True)] + [(OWN0 + 128 * m, 128, False)
                                                     for m in range(16)]
                    tensor_gate = None
                    for (i0, iw, ragged) in ichunks:
                        if ragged:
                            jlist = [0, 1, 2, 3, 4]
                        else:
                            m = (i0 - OWN0) // 128
                            jlist = [m + 1, m + 2, m + 3, m + 4, m + 5]
                        av_done = []
                        egate = [None, None]
                        for h in range(HEADS):
                            par = h % 2
                            qcc = h // 2
                            qrow = 64 * (h % 2)
                            exsl = EXQ[:, 640 * par:640 * (par + 1)]
                            sc_toks = []
                            for ji, jc in enumerate(jlist):
                                bank = (pb[par][:, 128 * ji:128 * ji + iw] if ji < 4
                                        else pb[2 + par][:, 0:iw])
                                lhs = KBf[qrow:qrow + 64,
                                          qcc * KTB + 128 * jc:qcc * KTB + 128 * jc + 128]
                                rhs = QB[qrow:qrow + 64,
                                         qcc * TB + i0:qcc * TB + i0 + iw]
                                w0 = [egate[par]] if ji == 0 else []
                                if ji == 0 and tensor_gate is not None:
                                    w0 = w0 + [tensor_gate]
                                sc_toks.append(kb.mm(bank, [(lhs, rhs, w0)],
                                                     wait_first=zt if (h == 0 and ji == 0)
                                                     else []))
                            if iw == 128:
                                e1 = kb.emit("scalar", "activation", exsl[:, 0:512],
                                             pb[par][:, 0:512], AF.Exp,
                                             wait=sc_toks[0:4] + av_done[-2:-1], inc="sS")
                            else:
                                e1 = kb.emit("scalar", "activation",
                                             exsl[:, 0:4 * iw]
                                             .rearrange("p (a b) -> p a b", b=iw),
                                             pb[par][:, 0:512]
                                             .rearrange("p (a b) -> p a b", b=128)
                                             [:, :, 0:iw],
                                             AF.Exp, wait=sc_toks[0:4] + av_done[-2:-1],
                                             inc="sS")
                            e2 = kb.emit("scalar", "activation", exsl[:, 4 * iw:5 * iw],
                                         pb[2 + par][:, 0:iw], AF.Exp, wait=[sc_toks[4]],
                                         inc="sS")
                            egate[par] = e2
                            mt = []
                            ml, mu_ = (MRL, MRU) if ragged else (MTL, MTU)
                            mt.append(kb.emit("vector", "tensor_tensor", exsl[:, 0:iw],
                                              exsl[:, 0:iw], ml[:, 0:iw], ALU.mult,
                                              wait=[e1], inc="sV"))
                            mt.append(kb.emit("vector", "tensor_tensor",
                                              exsl[:, 4 * iw:5 * iw], exsl[:, 4 * iw:5 * iw],
                                              mu_[:, 0:iw], ALU.mult, wait=[e2], inc="sV"))
                            for ji, jc in enumerate(jlist):
                                if jc <= 4:
                                    mt.append(kb.emit(
                                        "vector", "tensor_scalar",
                                        exsl[:, ji * iw:(ji + 1) * iw],
                                        exsl[:, ji * iw:(ji + 1) * iw], CMK[:, 0:1], None,
                                        ALU.mult, wait=[e1 if ji < 4 else e2], inc="sV"))
                            ob = pb[4 + h // 4][0:65, 128 * (h % 4):128 * (h % 4) + iw]
                            last = None
                            for ji, jc in enumerate(jlist):
                                last = kb.emit(
                                    "tensor", "matmul", ob,
                                    VB[:, jc * VSTR + 65 * h:jc * VSTR + 65 * (h + 1)],
                                    exsl[:, ji * iw:(ji + 1) * iw],
                                    start=(ji == 0), stop=(ji == 4),
                                    wait=(mt + [e1, e2]) if ji == 0 else [],
                                    inc="sT" if ji == 4 else None)
                            av_done.append(last)
                        s1 = kb.emit("scalar", "activation", SC[0:1, 0:512],
                                     pb[4][64:65, 0:512], AF.Copy, bias=1e-30,
                                     wait=av_done, inc="sS")
                        s2 = kb.emit("scalar", "activation", SC[0:1, 512:1024],
                                     pb[5][64:65, 0:512], AF.Copy, bias=1e-30, inc="sS")
                        r1 = kb.emit("vector", "reciprocal", SC[0:1, 0:1024],
                                     SC[0:1, 0:1024], wait=[s1, s2], inc="sV")
                        ogate = None
                        for oc in range(4):
                            h0, h1 = 2 * oc, 2 * oc + 1
                            b1 = kb.mm(pb[6][0:64, 0:iw],
                                       [(ONE64[:, :], SC[0:1, 128 * h0:128 * h0 + iw],
                                         [r1])], wait_first=[ogate])
                            b2 = kb.mm(pb[6][0:64, 128:128 + iw],
                                       [(ONE64[:, :], SC[0:1, 128 * h1:128 * h1 + iw],
                                         [r1])])
                            rb1 = kb.emit("scalar", "activation", SCB[0:64, 0:iw],
                                          pb[6][0:64, 0:iw], AF.Copy, wait=[b1], inc="sS")
                            rb2 = kb.emit("scalar", "activation", SCB[64:128, 0:iw],
                                          pb[6][0:64, 128:128 + iw], AF.Copy, wait=[b2],
                                          inc="sS")
                            ogate = rb2
                            o1 = kb.emit("vector", "tensor_tensor",
                                         QB[0:64, oc * TB + i0:oc * TB + i0 + iw],
                                         pb[4 + h0 // 4][0:64,
                                                         128 * (h0 % 4):128 * (h0 % 4) + iw],
                                         SCB[0:64, 0:iw], ALU.mult, wait=[rb1], inc="sV")
                            o2 = kb.emit("vector", "tensor_tensor",
                                         QB[64:128, oc * TB + i0:oc * TB + i0 + iw],
                                         pb[4 + h1 // 4][0:64,
                                                         128 * (h1 % 4):128 * (h1 % 4) + iw],
                                         SCB[64:128, 0:iw], ALU.mult, wait=[rb2], inc="sV")
                        tensor_gate = o2
                    kb.barrier()
                    wd2 = [None, None]
                    bk2 = [None, None]
                    for ocx in range(NCH):
                        slot = ocx % 2
                        wt = kb.dma("gpsimd", WQD[:, 1024 * slot:1024 * slot + 512]
                                    .rearrange("p (c f) -> p c f", f=128),
                                    wo[l][:, 128 * ocx:128 * (ocx + 1)]
                                    .rearrange("(c p) f -> p c f", p=128),
                                    wait=[wd2[slot]])
                        for (ts0, tw) in TCH:
                            bank = pb[slot][:, 0:tw]
                            mt = kb.mm(bank,
                                       [(WQD[:, 1024 * slot + 128 * cc:
                                             1024 * slot + 128 * (cc + 1)],
                                         QB[:, cc * TB + ts0:cc * TB + ts0 + tw])
                                        for cc in range(4)],
                                       wait_first=[wt, bk2[slot]])
                            bk2[slot] = kb.emit("vector", "scalar_tensor_tensor",
                                                _ch(XR, ocx, ts0, tw), bank, 1.0,
                                                _ch(XR, ocx, ts0, tw), ALU.mult, ALU.add,
                                                wait=[mt], inc="sV")
                            wd2[slot] = bk2[slot]
                    halo_fix()
                    kb.barrier()
                if _L1SUB == 1:
                    return
                for (ts0, tw) in TCH:
                    for c in range(NCH):
                        src = _ch(XR, c, ts0 - (1 if c >= 4 else 0), tw)
                        kb.emit("scalar", "activation", _ch(ACT, c, ts0, tw), src, AF.Copy,
                                inc="sS")
                kb.barrier()
                layernorm(ACT, ACT, affine_l=l, src_is_bf=True)
                kb.barrier()
                with ExitStack() as ss:
                    PBF = ss.enter_context(sb("PBF", [128, 22 * 1040], BF16))
                    W1D = ss.enter_context(sb("W1D", [128, 4 * 1024], BF16))
                    W2D = ss.enter_context(sb("W2D", [128, 2 * 2816], BF16))
                    for blk in TBLK:
                        b0 = blk[0][0]
                        w1_used = [None, None]
                        ga_gate = [None, None]
                        bkA = [None, None]
                        bkG = [None, None]
                        lastp = {}
                        for pc in range(22):
                            slot = pc % 2
                            wta = kb.dma("gpsimd",
                                         W1D[:, 2048 * slot:2048 * slot + 1024]
                                         .rearrange("p (c f) -> p c f", f=128),
                                         w1p[l][:, 128 * pc:128 * (pc + 1)]
                                         .rearrange("(c p) f -> p c f", p=128),
                                         wait=[w1_used[slot]])
                            wtg = kb.dma("gpsimd",
                                         W1D[:, 2048 * slot + 1024:2048 * slot + 2048]
                                         .rearrange("p (c f) -> p c f", f=128),
                                         w1p[l][:, HIDP + 128 * pc:HIDP + 128 * (pc + 1)]
                                         .rearrange("(c p) f -> p c f", p=128))
                            for (ts0, tw) in blk:
                                bA = pb[slot][:, 0:tw]
                                bG = pb[2 + slot][:, 0:tw]
                                mta = kb.mm(bA, [(W1D[:, 2048 * slot + 128 * cc:
                                                      2048 * slot + 128 * (cc + 1)],
                                                  _ch(ACT, cc, ts0, tw))
                                                 for cc in range(NCH)],
                                            wait_first=[wta, bkA[slot]])
                                mtg = kb.mm(bG, [(W1D[:, 2048 * slot + 1024 + 128 * cc:
                                                      2048 * slot + 1024 + 128 * (cc + 1)],
                                                  _ch(ACT, cc, ts0, tw))
                                                 for cc in range(NCH)],
                                            wait_first=[wtg, bkG[slot]])
                                gtile = SCB[:, 512 * slot:512 * slot + tw]
                                ga = kb.emit("scalar", "activation", gtile, bG, AF.Gelu,
                                             bias=B1S[:, 44 * l + 22 + pc:44 * l + 23 + pc],
                                             wait=[mtg, ga_gate[slot]], inc="sS")
                                pt = kb.emit("vector", "scalar_tensor_tensor",
                                             PBF[:, pc * 1040 + ts0 - b0:
                                                 pc * 1040 + ts0 - b0 + tw],
                                             bA, B1S[:, 44 * l + pc:44 * l + 1 + pc],
                                             gtile, ALU.add, ALU.mult, wait=[ga, mta],
                                             inc="sV")
                                bkA[slot] = pt
                                bkG[slot] = pt
                                ga_gate[slot] = pt
                                w1_used[slot] = pt
                                lastp[pc] = pt
                        ptok_all = [lastp[pc] for pc in range(22)]
                        w2_used = [None, None]
                        bkO = [None, None]
                        for ocx in range(NCH):
                            slot = ocx % 2
                            wt2 = kb.dma("gpsimd",
                                         W2D[:, 2816 * slot:2816 * (slot + 1)]
                                         .rearrange("p (c f) -> p c f", f=128),
                                         w2p[l][:, 128 * ocx:128 * (ocx + 1)]
                                         .rearrange("(c p) f -> p c f", p=128),
                                         wait=[w2_used[slot]])
                            for (ts0, tw) in blk:
                                bank = pb[4 + slot][:, 0:tw]
                                mt = kb.mm(bank,
                                           [(W2D[:, 2816 * slot + 128 * pc:
                                                 2816 * slot + 128 * (pc + 1)],
                                             PBF[:, pc * 1040 + ts0 - b0:
                                                 pc * 1040 + ts0 - b0 + tw])
                                            for pc in range(22)],
                                           wait_first=[wt2, bkO[slot]] + ptok_all)
                                bkO[slot] = kb.emit(
                                    "vector", "scalar_tensor_tensor",
                                    _ch(XR, ocx, ts0, tw), bank,
                                    B2S[:, NCH * l + ocx:NCH * l + ocx + 1],
                                    _ch(XR, ocx, ts0, tw), ALU.add, ALU.add, wait=[mt],
                                    inc="sV")
                                w2_used[slot] = bkO[slot]
                halo_fix()
                kb.barrier()

            def l1_ffn():
                sp = kb.dma("sync", spill[:, :], XR[:, :])
                for e in ("vector", "scalar", "tensor"):
                    kb.wait_only(e, [sp])
                layernorm(XR, XR)
                kb.barrier()
                if _L1SUB == 9:
                    return
                l1_layer(0)
                if _L1SUB in (1, 2):
                    return
                l1_layer(1)
                if _L1SUB == 3:
                    return
                dgate = None
                _comb = 0.0 if _L1SUB == 5 else 0.5
                for (ts0, tw) in TCH:
                    for c in range(NCH):
                        tile = SC[:, 2072:2072 + tw]
                        dt_ = kb.dma("sync", tile,
                                     spill[:, c * TB + ts0:c * TB + ts0 + tw],
                                     wait=[dgate])
                        dgate = kb.emit("vector", "scalar_tensor_tensor",
                                        _ch(XR, c, ts0, tw), _ch(XR, c, ts0, tw), _comb,
                                        tile, ALU.mult, ALU.add, wait=[dt_], inc="sV")
                halo_fix()
                kb.barrier()

            def l1_attn():
                layernorm(XR, ACT)
                kb.barrier()
                with ExitStack() as ss:
                    CVB = ss.enter_context(sb("CVB", [128, NCH * TB], BF16))
                    WQD = ss.enter_context(sb("WQDa", [128, 2 * 1024], BF16))
                    QF = ss.enter_context(sb("QF", [128, 512], BF16))
                    KF = ss.enter_context(sb("KF", [128, 512], BF16))
                    COEF = ss.enter_context(sb("COEF", [16, TB], BF16))
                    wq_t = None
                    wk_t = None
                    cm_tok = {}
                    prgate = None
                    prev_cm = None
                    gate_q = None
                    gate_k = None
                    wq_gate = None
                    wk_gate = None
                    for c in range(NCH):
                        wq_t = kb.dma("gpsimd", WQD[:, 0:1024]
                                      .rearrange("p (c f) -> p c f", f=128),
                                      l1awq[:, 128 * c:128 * (c + 1)]
                                      .rearrange("(c p) f -> p c f", p=128),
                                      wait=[wq_t, wq_gate])
                        wk_t = kb.dma("gpsimd", WQD[:, 1024:2048]
                                      .rearrange("p (c f) -> p c f", f=128),
                                      l1awk[:, 128 * c:128 * (c + 1)]
                                      .rearrange("(c p) f -> p c f", p=128),
                                      wait=[wk_t, wk_gate])
                        for ti, (ts0, tw) in enumerate(TCH):
                            bq_ = pb[0][:, 0:tw]
                            bk_ = pb[1][:, 0:tw]
                            mq = kb.mm(bq_, [(WQD[:, 128 * cc:128 * (cc + 1)],
                                              _ch(ACT, cc, ts0, tw)) for cc in range(NCH)],
                                       wait_first=[wq_t, gate_q])
                            mk = kb.mm(bk_, [(WQD[:, 1024 + 128 * cc:1024 + 128 * (cc + 1)],
                                              _ch(ACT, cc, ts0, tw)) for cc in range(NCH)],
                                       wait_first=[wk_t, gate_k])
                            sq = kb.emit("scalar", "activation", SC[:, 2072:2072 + tw], bq_,
                                         AF.Square, bias=LAB[:, c:c + 1],
                                         wait=[mq, prgate], inc="sS")
                            gate_q = sq
                            wq_gate = sq
                            q1 = kb.emit("scalar", "activation", QF[:, 0:tw],
                                         SC[:, 2072:2072 + tw], AF.Exp, scale=-0.5,
                                         wait=[sq, prev_cm], inc="sS")
                            cm_tok[("q", ti, c)] = q1
                            sk = kb.emit("scalar", "activation", SC[:, 2072:2072 + tw], bk_,
                                         AF.Square, bias=LAB[:, NCH + c:NCH + c + 1],
                                         wait=[mk, q1], inc="sS")
                            k1 = kb.emit("scalar", "activation", KF[:, 0:tw],
                                         SC[:, 2072:2072 + tw], AF.Exp, scale=-0.5,
                                         wait=[sk], inc="sS")
                            gate_k = sk
                            wk_gate = sk
                            cm_tok[("k", ti, c)] = k1
                            pr = kb.emit("vector", "tensor_tensor", QF[:, 0:tw], QF[:, 0:tw],
                                         KF[:, 0:tw], ALU.mult, wait=[q1, k1], inc="sV")
                            prgate = pr
                            cm = kb.emit("tensor", "matmul", pb[2 + ti][0:16, 0:tw],
                                         BH16S[:, 16 * c:16 * (c + 1)], QF[:, 0:tw],
                                         start=(c == 0), stop=(c == NCH - 1), wait=[pr],
                                         inc="sT")
                            cm_tok[("q", ti, c)] = cm
                            cm_tok[("k", ti, c)] = cm
                            prev_cm = cm
                            if c == NCH - 1:
                                kb.emit("scalar", "activation", COEF[0:16, ts0:ts0 + tw],
                                        pb[2 + ti][0:16, 0:tw], AF.Copy, wait=[cm],
                                        inc="sS")
                    kb.barrier()
                    if dbg is not None:
                        kb.dma("gpsimd", dbg[0:16, 0:512], COEF[0:16, OWN0:OWN0 + 512])
                        kb.dma("gpsimd", dbg[16:128, 256:512], ACT[0:112, OWN0:OWN0 + 256])
                    wv_t = None
                    bkv = None
                    bbg = None
                    for c in range(NCH):
                        wv_t = kb.dma("gpsimd", WQD[:, 0:1024]
                                      .rearrange("p (c f) -> p c f", f=128),
                                      l1awv[:, 128 * c:128 * (c + 1)]
                                      .rearrange("(c p) f -> p c f", p=128),
                                      wait=[wv_t, bkv])
                        for (ts0, tw) in TCH:
                            bank = pb[0][:, 0:tw]
                            mv = kb.mm(bank, [(WQD[:, 128 * cc:128 * (cc + 1)],
                                               _ch(ACT, cc, ts0, tw)) for cc in range(NCH)],
                                       wait_first=[wv_t, bkv])
                            bb = kb.mm(pb[1][:, 0:tw],
                                       [(BH16TS[0:16, 128 * c:128 * (c + 1)],
                                         COEF[0:16, ts0:ts0 + tw])], wait_first=[bbg])
                            vt = kb.emit("vector", "tensor_scalar", SCB[:, 0:tw], bank,
                                         LAB[:, 2 * NCH + c:2 * NCH + c + 1], None,
                                         ALU.add, wait=[mv], inc="sV")
                            cvt = kb.emit("vector", "tensor_tensor", _ch(CVB, c, ts0, tw),
                                          SCB[:, 0:tw], pb[1][:, 0:tw], ALU.mult,
                                          wait=[vt, bb], inc="sV")
                            bkv = vt
                            bbg = cvt
                        wv_t = vt
                    kb.barrier()
                    wo_t = [None, None]
                    bko = [None, None]
                    for ocx in range(NCH):
                        slot = ocx % 2
                        wt = kb.dma("gpsimd", WQD[:, 1024 * slot:1024 * (slot + 1)]
                                    .rearrange("p (c f) -> p c f", f=128),
                                    l1awo[:, 128 * ocx:128 * (ocx + 1)]
                                    .rearrange("(c p) f -> p c f", p=128),
                                    wait=[wo_t[slot]])
                        for (ts0, tw) in TCH:
                            bank = pb[2 + slot][:, 0:tw]
                            mt = kb.mm(bank, [(WQD[:, 1024 * slot + 128 * cc:
                                                   1024 * slot + 128 * (cc + 1)],
                                               _ch(CVB, cc, ts0, tw)) for cc in range(NCH)],
                                       wait_first=[wt, bko[slot]])
                            bko[slot] = kb.emit(
                                "vector", "scalar_tensor_tensor", _ch(XR, ocx, ts0, tw),
                                bank, LAB[:, 3 * NCH + ocx:3 * NCH + ocx + 1],
                                _ch(XR, ocx, ts0, tw), ALU.add, ALU.add, wait=[mt],
                                inc="sV")
                            wo_t[slot] = bko[slot]
                halo_fix()
                kb.barrier()

            stages = [l0_ffn, l0_attn, l0_ffn, l1_ffn, l1_attn, l1_ffn]
            for i, s in enumerate(stages):
                if i < nstage:
                    s()

            for c in range(NCH):
                kb.dma("sync", yT[128 * c:128 * (c + 1), :],
                       XR[:, c * TB + OWN0:c * TB + OWN0 + TPC])
            kb.wait_only("sync", [("dY", kb.cnt["dY"])])
    return nc


def _host_inputs(inputs):
    x = np.asarray(inputs["L0_attn_x"], np.float32)[0]
    xT = np.ascontiguousarray(x.T)
    pi = np.pi

    def resh_bias(b):
        return np.ascontiguousarray(np.asarray(b, np.float32).reshape(NCH, 128).T)

    base = {}
    base["cst"] = np.tile(np.array(CONSTS, np.float32), (128, 1))
    base["l0wrT"] = np.ascontiguousarray(np.asarray(inputs["L0_ffn_W_r"], np.float32).T)
    base["l0wkT"] = np.ascontiguousarray(np.asarray(inputs["L0_ffn_W_k"], np.float32).T)
    base["l0wvT"] = np.ascontiguousarray(np.asarray(inputs["L0_ffn_W_v"], np.float32).T)
    base["l0mur"] = resh_bias(inputs["L0_ffn_mu_r"])
    base["l0muk"] = resh_bias(inputs["L0_ffn_mu_k"])
    base["l0wq"] = np.asarray(inputs["L0_attn_Wq"], np.float32)
    base["l0wk"] = np.asarray(inputs["L0_attn_Wk"], np.float32)
    base["l0wv"] = np.asarray(inputs["L0_attn_Wv"], np.float32)
    base["l1awq"] = np.asarray(inputs["L1_attn_Wq"], np.float32)
    base["l1awk"] = np.asarray(inputs["L1_attn_Wk"], np.float32)
    base["l1awv"] = np.asarray(inputs["L1_attn_Wv"], np.float32)
    base["l1awo"] = np.asarray(inputs["L1_attn_Wo"], np.float32)
    base["l1abq"] = resh_bias(inputs["L1_attn_bq"])
    base["l1abk"] = resh_bias(inputs["L1_attn_bk"])
    base["l1abv"] = resh_bias(inputs["L1_attn_bv"])
    base["l1abo"] = resh_bias(inputs["L1_attn_bo"])
    base["wqkv"] = np.asarray(inputs["L1_ffn_wqkv"], np.float32)
    base["wo"] = np.asarray(inputs["L1_ffn_wo"], np.float32)
    w1 = np.asarray(inputs["L1_ffn_w1"], np.float32)
    HID = 2730
    w1pad = np.zeros((2, D, HID2), np.float32)
    w1pad[:, :, 0:HID] = w1[:, :, 0:HID]
    w1pad[:, :, HIDP:HIDP + HID] = w1[:, :, HID:2 * HID]
    base["w1p"] = w1pad
    b1 = np.asarray(inputs["L1_ffn_b1"], np.float32)
    b1pad = np.zeros((2, HID2), np.float32)
    b1pad[:, 0:HID] = b1[:, 0:HID]
    b1pad[:, HIDP:HIDP + HID] = b1[:, HID:]
    base["b1p"] = np.stack([np.ascontiguousarray(b1pad[l].reshape(44, 128).T)
                            for l in range(2)])
    w2 = np.asarray(inputs["L1_ffn_w2"], np.float32)
    w2pad = np.zeros((2, HIDP, D), np.float32)
    w2pad[:, 0:HID, :] = w2
    base["w2p"] = w2pad
    base["b2r"] = np.stack([resh_bias(np.asarray(inputs["L1_ffn_b2"], np.float32)[l])
                            for l in range(2)])
    lng = np.asarray(inputs["L1_ffn_ln_g"], np.float32)
    base["lng_pp"] = np.stack([resh_bias(lng[l]) for l in range(2)])
    base["lnbr"] = np.stack([resh_bias(np.asarray(inputs["L1_ffn_ln_b"], np.float32)[l])
                             for l in range(2)])
    qs = np.asarray(inputs["L1_ffn_q_scale"], np.float32)
    ks = np.asarray(inputs["L1_ffn_k_scale"], np.float32)
    base["qscal"] = np.ascontiguousarray(np.tile(qs, (1, 2)).T)
    base["kscal"] = np.ascontiguousarray(np.tile(ks, (1, 2)).T)
    p_ = np.arange(128)[:, None]
    ic_ = np.arange(128)[None, :]
    base["mtril"] = (ic_ <= p_).astype(np.float32)
    base["mtriu"] = (ic_ >= p_).astype(np.float32)
    ich = np.arange(HALO)[None, :]
    base["mrl"] = (p_ >= 112 + ich).astype(np.float32)
    base["mru"] = (p_ <= 112 + ich).astype(np.float32)
    bh16 = np.zeros((NCH, 128, 16), np.float32)
    for c in range(NCH):
        for p in range(128):
            bh16[c, p, 2 * c + p // 64] = 1.0
    base["bh16"] = bh16
    base["bh16T"] = np.ascontiguousarray(np.transpose(bh16, (0, 2, 1)))
    bh8 = np.zeros((4, 128, 8), np.float32)
    for c in range(4):
        for p in range(128):
            bh8[c, p, 2 * c + p // 64] = 1.0
    base["bh8"] = bh8
    base["bh8T"] = np.ascontiguousarray(np.transpose(bh8, (0, 2, 1)))
    rotm = np.zeros((128, 128), np.float32)
    for b in range(2):
        for d_ in range(32):
            rotm[b * 64 + d_ + 32, b * 64 + d_] = -1.0
            rotm[b * 64 + d_, b * 64 + d_ + 32] = 1.0
    base["rotm"] = rotm
    half_idx = np.arange(0, DH, 2, dtype=np.float64)
    inv_freq = 1.0 / 10000.0 ** (half_idx / DH)
    invf64 = np.concatenate([inv_freq, inv_freq])
    bs = (half_idx + 0.4 * DH) / (1.4 * DH)
    lnb = np.log(np.concatenate([bs, bs]))
    base["rlnb"] = lnb.astype(np.float32).reshape(1, 64)
    base["rlnbn"] = (-lnb).astype(np.float32).reshape(1, 64)

    maps = []
    for core in range(NCORE):
        m = dict(base)
        t0 = core * TPC
        lo = t0 - OWN0
        xin = np.zeros((D, TB), np.float32)
        s0 = max(lo, 0)
        xin[:, s0 - lo:] = xT[:, s0:t0 + TPC]
        m["xin"] = xin
        m["cmask"] = np.full((128, 1), 0.0 if core == 0 else 1.0, np.float32)
        g = lo + np.arange(TB, dtype=np.float64)
        ang = g[None, :] * invf64[:, None]
        m["rfc"] = (np.mod(ang + pi / 2 + pi, 2 * pi) - pi).astype(np.float32)
        m["rfs"] = (np.mod(ang + pi, 2 * pi) - pi).astype(np.float32)
        m["rpow"] = ((g - TOK // 2) / (WIN // 2)).astype(np.float32).reshape(1, TB)
        maps.append(m)
    return maps


def kernel(**inputs):
    maps = _host_inputs(inputs)
    nc = build_program(_NSTAGE)
    res = run_bass_kernel_spmd(nc, maps, core_ids=list(range(NCORE)))
    outs = [np.asarray(res.results[c]["yT"], np.float32).T for c in range(NCORE)]
    y = np.concatenate(outs, axis=0).reshape(1, TOK, D)
    if _DBG:
        kernel._dbg = [np.asarray(res.results[c]["dbg"]) for c in range(NCORE)]
    return y
